# version 02
# baseline (speedup 1.0000x reference)
import numpy as np
import ml_dtypes
from contextlib import ExitStack

import concourse.bass as bass
import concourse.mybir as mybir
from concourse.bass_utils import run_bass_kernel_spmd

F32 = mybir.dt.float32
BF16 = mybir.dt.bfloat16
U32 = mybir.dt.uint32
AF = mybir.ActivationFunctionType
ALU = mybir.AluOpType
IOA = bass.IndirectOffsetOnAxis
AXX = mybir.AxisListType.X

D = 2048
G = 16
TT = 512
NBLK = 4
TOK = 4096
W = 512
NTA = 9
NTB = 8
E = 8
HID = 1024
HG = 8
CAP = 1280
NROUT = E * CAP
NCT = E * CAP + TOK
LNEPS = 1e-5
BIG = 65536.0
ARENA_B = 206816
OPTS = {}


class Ctr:
    def __init__(self, h):
        self.h = h
        self.n = 0

    def inc(self, instr, k=1):
        instr.then_inc(self.h, k)
        self.n += k
        return self.n


def build(debug=False):
    UPTO_N = {"A": 0, "B1": 1, "B2": 2, "C": 3, "D": 4}[OPTS.get("upto", "D")]
    nc = bass.Bass("TRN2", target_bir_lowering=False, debug=False)

    x_in = nc.dram_tensor("x", [W + TOK, D], F32, kind="ExternalInput").ap()
    wkT = nc.dram_tensor("wkT", [D, D], F32, kind="ExternalInput").ap()
    wvT = nc.dram_tensor("wvT", [D, D], F32, kind="ExternalInput").ap()
    wrT = nc.dram_tensor("wrT", [D, D], F32, kind="ExternalInput").ap()
    woT = nc.dram_tensor("woT", [D, D], F32, kind="ExternalInput").ap()
    w1T = nc.dram_tensor("w1T", [E, D, HID], F32, kind="ExternalInput").ap()
    w2T = nc.dram_tensor("w2T", [E, HID, D], F32, kind="ExternalInput").ap()
    w3T = nc.dram_tensor("w3T", [E, D, HID], F32, kind="ExternalInput").ap()
    sw1T = nc.dram_tensor("sw1T", [D, HID], F32, kind="ExternalInput").ap()
    sw2T = nc.dram_tensor("sw2T", [HID, D], F32, kind="ExternalInput").ap()
    sw3T = nc.dram_tensor("sw3T", [D, HID], F32, kind="ExternalInput").ap()
    gw_pk = nc.dram_tensor("gw_pk", [128, G * E], F32, kind="ExternalInput").ap()
    tmk_i = nc.dram_tensor("tmk", [128, G], F32, kind="ExternalInput").ap()
    tmv_i = nc.dram_tensor("tmv", [128, G], F32, kind="ExternalInput").ap()
    tmr_i = nc.dram_tensor("tmr", [128, G], F32, kind="ExternalInput").ap()
    dec_i = nc.dram_tensor("decay", [128, G], F32, kind="ExternalInput").ap()
    tf_i = nc.dram_tensor("tfirst", [128, G], F32, kind="ExternalInput").ap()
    mask_i = nc.dram_tensor("mask", [128, 1], F32, kind="ExternalInput").ap()

    out_t = nc.dram_tensor("out", [TOK, D], F32, kind="ExternalOutput").ap()

    skind = "ExternalOutput" if debug else "Internal"
    xlnfm_b = nc.dram_tensor("xlnfm_b", [D, NTA * TT], BF16, kind=skind).ap()
    wkv_b = nc.dram_tensor("wkv_b", [D, TOK], BF16, kind=skind).ap()
    wkvsr_b = nc.dram_tensor("wkvsr_b", [D, TOK], BF16, kind=skind).ap()
    x1_b = nc.dram_tensor("x1_b", [TOK, D], F32, kind=skind).ap()
    xt_b = nc.dram_tensor("xt_b", [TOK, D], BF16, kind=skind).ap()
    routed_b = nc.dram_tensor("routed_b", [NROUT, D], BF16, kind=skind).ap()
    contrib_b = nc.dram_tensor("contrib_b", [NCT, D], BF16, kind=skind).ap()
    dbg_idx = (nc.dram_tensor("dbg_idx", [128, 72], mybir.dt.uint32, kind="ExternalOutput").ap()
               if debug else None)
    dbg_stat = (nc.dram_tensor("dbg_stat", [128, 192], F32, kind="ExternalOutput").ap()
                if debug else None)
    dbg_rsc = (nc.dram_tensor("dbg_rsc", [128, 128], F32, kind="ExternalOutput").ap()
               if debug else None)
    dbg_ar = (nc.dram_tensor("dbg_ar", [128, 4096], F32, kind="ExternalOutput").ap()
              if debug else None)

    es = ExitStack()
    with es:
        AR = es.enter_context(nc.sbuf_tensor("AR", [128, ARENA_B // 2], BF16))
        CONST = es.enter_context(nc.sbuf_tensor("CONST", [128, 128], F32))
        GW32 = es.enter_context(nc.sbuf_tensor("GW32", [128, G * E], F32))
        GWHI = es.enter_context(nc.sbuf_tensor("GWHI", [128, G * E], BF16))
        GWLO = es.enter_context(nc.sbuf_tensor("GWLO", [128, G * E], BF16))
        STAT = es.enter_context(nc.sbuf_tensor("STAT", [128, 192], F32))
        RSC = es.enter_context(nc.sbuf_tensor("RSC", [128, 128], F32))
        IDXS = es.enter_context(nc.sbuf_tensor("IDXS", [128, 72], U32))
        ID16 = es.enter_context(nc.sbuf_tensor("ID16", [128, 128], BF16))
        ID32 = es.enter_context(nc.sbuf_tensor("ID32", [128, 128], F32))
        LNSCR = es.enter_context(nc.sbuf_tensor("LNSCR", [128, 512], F32))

        PF = [es.enter_context(nc.psum_tensor(f"pf{i}", [128, 512], F32))
              for i in range(6)]
        PB = [es.enter_context(nc.psum_tensor(f"pb{i}", [128, 1024], BF16))
              for i in range(2)]

        def sem(name):
            return Ctr(es.enter_context(nc.semaphore(name)))

        m_pe = sem("m_pe")
        m_dve = sem("m_dve")
        m_act = sem("m_act")
        m_gps = sem("m_gps")
        ld_w = sem("ld_w")
        ld_x = sem("ld_x")
        ld_xc = sem("ld_xc")
        zr = sem("zr")
        ld_g = sem("ld_g")
        st_a = sem("st_a")
        ld_b = sem("ld_b")
        st_b = sem("st_b")
        sc_r = sem("sc_r")
        ld_c = sem("ld_c")
        st_c = sem("st_c")
        ld_d = sem("ld_d")
        st_o = sem("st_o")

        blk = es.enter_context(nc.Block())
        TEN, VEC, ACT, GPS, SYN = nc.tensor, nc.vector, nc.scalar, nc.gpsimd, nc.sync

        from concourse.hw_specs import get_activation_tables
        _setnames = list(get_activation_tables(nc.m.arch).keys())
        _LNEXP_ID = _setnames.index("natural_log_exp_and_others")

        def load_lnexp_table():
            return ACT.add_instruction(mybir.InstLoadActFuncSet(
                name=nc.get_next_instruction_name(),
                act_func_set_id=_LNEXP_ID, ins=[], outs=[]))

        def vb(off, cols):
            assert off % 2 == 0
            return AR[:, off // 2: off // 2 + cols]

        def vf(off, cols):
            assert off % 4 == 0
            return AR[:, off // 2: off // 2 + 2 * cols].bitcast(F32)

        c_lam = CONST[:, 0:G]
        c_tmk = CONST[:, 16:32]
        c_tmv = CONST[:, 32:48]
        c_tmr = CONST[:, 48:64]
        c_u = CONST[:, 64:80]
        c_mask = CONST[:, 80:81]
        c_one = CONST[:, 81:82]
        c_eps = CONST[:, 82:83]
        c_iotaB = CONST[:, 88:96]
        c_iota = CONST[:, 96:104]
        c_dec = CONST[:, 104:120]

        s_ssum = STAT[:, 0:16]
        s_ssq = STAT[:, 16:32]
        s_sum4 = STAT[:, 32:36]
        s_ex4 = STAT[:, 36:40]
        s_m4 = STAT[:, 40:44]
        s_v4 = STAT[:, 44:48]
        s_r4 = STAT[:, 48:52]
        s_nb4 = STAT[:, 52:56]
        s_ca = STAT[:, 56:72]
        s_cb = STAT[:, 72:88]
        s_bnd = [STAT[:, 88:104], STAT[:, 104:120]]
        s_d0 = STAT[:, 120:136]
        s_dk = STAT[:, 136:152]
        s_run = STAT[:, 152:153]
        s_m12 = STAT[:, 156:164]
        s_i1 = STAT[:, 164:168]
        s_i2 = STAT[:, 168:172]
        s_s12 = STAT[:, 172:180]

        r_eq = RSC[:, 0:8]
        r_cand = RSC[:, 8:16]
        r_lg2 = RSC[:, 16:24]
        r_msk1 = [RSC[:, 24 + 8 * b: 32 + 8 * b] for b in range(4)]
        r_msk2 = [RSC[:, 56 + 8 * b: 64 + 8 * b] for b in range(4)]
        r_ptm = RSC[:, 88:96]
        r_pa = RSC[:, 96:100]
        r_pb = RSC[:, 100:104]
        r_pac = RSC[:, 104:108]
        r_pbc = RSC[:, 108:112]
        r_sl = RSC[:, 112:116]
        r_sl2 = RSC[:, 116:120]

        i_slotA = IDXS[:, 0:32]
        i_slotB = IDXS[:, 32:64]
        i_iota = IDXS[:, 64:72]

        nc.gpsimd.sem_clear(range(152, 256))
        nc.all_engine_barrier()

        GPS.memset(ID16[:, :], 0.0)
        GPS.affine_select(out=ID16[:, :], in_=ID16[:, :], compare_op=ALU.not_equal,
                          fill=1.0, base=0, pattern=[[-1, 128]], channel_multiplier=1)
        GPS.memset(ID32[:, :], 0.0)
        GPS.affine_select(out=ID32[:, :], in_=ID32[:, :], compare_op=ALU.not_equal,
                          fill=1.0, base=0, pattern=[[-1, 128]], channel_multiplier=1)
        GPS.iota(i_iota[:, :], pattern=[[1, 8]], base=0, channel_multiplier=0)
        n_gps_pre = m_gps.inc(GPS.tensor_copy(c_iota, i_iota[:, :]))

        ld_x.inc(SYN.dma_start(out=c_tmk, in_=tmk_i[:, :]), 16)
        ld_x.inc(SYN.dma_start(out=c_tmv, in_=tmv_i[:, :]), 16)
        ld_x.inc(SYN.dma_start(out=c_tmr, in_=tmr_i[:, :]), 16)
        ld_x.inc(SYN.dma_start(out=c_dec, in_=dec_i[:, :]), 16)
        ld_x.inc(SYN.dma_start(out=c_u, in_=tf_i[:, :]), 16)
        ld_x.inc(SYN.dma_start(out=c_mask, in_=mask_i[:, :]), 16)
        ld_x.inc(SYN.dma_start(out=GW32[:, :], in_=gw_pk[:, :]), 16)
        n_cld = ld_x.n

        ACT.wait_ge(ld_x.h, n_cld)
        load_lnexp_table()
        na = m_act.inc(ACT.activation(out=c_lam, in_=c_dec, func=AF.Exp))
        ACT.wait_ge(m_act.h, na)
        m_act.inc(ACT.activation(out=c_lam, in_=c_lam, func=AF.Exp, scale=-1.0))

        VEC.wait_ge(ld_x.h, n_cld)
        VEC.wait_ge(m_gps.h, n_gps_pre)
        VEC.memset(c_one, 1.0)
        VEC.memset(c_eps, LNEPS)
        VEC.tensor_scalar(out=c_iotaB, in0=c_iota, scalar1=BIG, scalar2=None,
                          op0=ALU.add)
        VEC.memset(s_ca[:, :], 0.0)
        VEC.memset(s_cb[:, :], 0.0)
        VEC.memset(s_bnd[0][:, :], 0.0)
        VEC.memset(s_run[:, :], 0.0)
        VEC.memset(IDXS[:, 0:64], 0)
        VEC.memset(STAT[:, 152:192], 0.0)
        VEC.memset(RSC[:, :], 0.0)
        nhi = m_dve.inc(VEC.tensor_copy(GWHI[:, :], GW32[:, :]))
        VEC.wait_ge(m_dve.h, nhi)
        m_dve.inc(VEC.tensor_tensor(out=GWLO[:, :], in0=GW32[:, :], in1=GWHI[:, :],
                                    op=ALU.subtract))

        zt = vb(0, 4096)
        nz = m_dve.inc(VEC.memset(zt[:, :], 0.0))
        SYN.wait_ge(m_dve.h, nz)
        zsrc = zt.rearrange("p (b d) -> p b d", b=2)
        for rr in range(0, NROUT, 256):
            dstz = routed_b[rr:rr + 256, :].rearrange("(b p) d -> p b d", p=128)
            n_zr = zr.inc(SYN.dma_start(out=dstz, in_=zsrc), 16)

        wk3 = vb(0, G * D).rearrange("p (i e) -> p i e", i=G)
        wv3 = vb(65536, G * D).rearrange("p (i e) -> p i e", i=G)
        GPS.wait_ge(zr.h, n_zr)
        for i in range(G):
            ld_w.inc(GPS.dma_start(out=wk3[:, i, :], in_=wkT[i * 128:(i + 1) * 128, :]), 16)
            ld_w.inc(GPS.dma_start(out=wv3[:, i, :], in_=wvT[i * 128:(i + 1) * 128, :]), 16)
        n_wA = ld_w.n

        A_xk, A_xv, A_xbf, A_stg, A_tmp = 131072, 147456, 163840, 180224, 184320
        xlnfm3 = vb(A_tmp, G * TT).rearrange("p (g t) -> p g t", g=G)
        t_q = [vf(A_tmp + 0, 512), vf(A_tmp + 2048, 512)]
        t_eu = [vf(A_tmp + 4096, 512), vf(A_tmp + 6144, 512)]
        t_p = vf(A_tmp + 8192, 512)
        t_euv = vf(A_tmp + 10240, 512)
        t_sa = vf(A_tmp + 12288, 513)
        t_sb = vf(A_tmp + 14344, 513)
        t_wkv = [vb(A_tmp + 16400, 512), vb(A_tmp + 17424, 512)]
        t_wq = vf(A_tmp + 18448, 512)
        t_dif = vb(A_tmp + 20500, 511)
        xbf4 = vb(A_xbf, 4 * D).rearrange("p (b d) -> p b d", b=4)
        xk3 = vb(A_xk, G * TT).rearrange("p (g t) -> p g t", g=G)
        xv3 = vb(A_xv, G * TT).rearrange("p (g t) -> p g t", g=G)
        stg = vb(A_stg, D)

        xblk_free = [0, 0, 0, 0]
        tp_free = [0, 0]
        kb_free = [0, 0]
        vb_free = [0, 0]
        wkv_war = [0, 0]
        last_mix_gps = n_gps_pre
        last_mm_pe = 0
        last_xln_st = 0
        last_tp_pe = 0

        def ln_stats_tile(blocks):
            for bi, bl in enumerate(blocks):
                for s in range(4):
                    sl = bl[:, s * 512:(s + 1) * 512]
                    ACT.activation(out=LNSCR[:, :], in_=sl, func=AF.Square,
                                   accum_out=s_ssq[:, bi * 4 + s: bi * 4 + s + 1])
                    ACT.activation(out=LNSCR[:, :], in_=sl, func=AF.Identity,
                                   accum_out=s_ssum[:, bi * 4 + s: bi * 4 + s + 1])
            return m_act.inc(ACT.activation(out=LNSCR[:, 0:1], in_=LNSCR[:, 0:1],
                                            func=AF.Identity))

        def ln_combine(nblk, n_acc):
            VEC.wait_ge(m_act.h, n_acc)
            ssum3 = s_ssum[:, 0:4 * nblk].rearrange("p (b s) -> p b s", b=nblk)
            ssq3 = s_ssq[:, 0:4 * nblk].rearrange("p (b s) -> p b s", b=nblk)
            cb = slice(0, nblk)
            VEC.tensor_reduce(out=s_sum4[:, cb], in_=ssum3, axis=AXX, op=ALU.add)
            ne = m_dve.inc(VEC.tensor_reduce(out=s_ex4[:, cb], in_=ssq3, axis=AXX,
                                             op=ALU.add))
            VEC.wait_ge(m_dve.h, ne)
            VEC.tensor_scalar(out=s_m4[:, cb], in0=s_sum4[:, cb], scalar1=1.0 / D,
                              scalar2=None, op0=ALU.mult)
            ne = m_dve.inc(VEC.tensor_scalar(out=s_ex4[:, cb], in0=s_ex4[:, cb],
                                             scalar1=1.0 / D, scalar2=None, op0=ALU.mult))
            VEC.wait_ge(m_dve.h, ne)
            ne = m_dve.inc(VEC.tensor_tensor(out=s_v4[:, cb], in0=s_m4[:, cb],
                                             in1=s_m4[:, cb], op=ALU.mult))
            VEC.wait_ge(m_dve.h, ne)
            ne = m_dve.inc(VEC.tensor_tensor(out=s_v4[:, cb], in0=s_ex4[:, cb],
                                             in1=s_v4[:, cb], op=ALU.subtract))
            ACT.wait_ge(m_dve.h, ne)
            na = m_act.inc(ACT.activation(out=s_r4[:, cb], in_=s_v4[:, cb],
                                          func=AF.Ln, bias=c_eps))
            ACT.wait_ge(m_act.h, na)
            na = m_act.inc(ACT.activation(out=s_r4[:, cb], in_=s_r4[:, cb],
                                          func=AF.Exp, scale=-0.5))
            VEC.wait_ge(m_act.h, na)
            ne = m_dve.inc(VEC.tensor_tensor(out=s_nb4[:, cb], in0=s_m4[:, cb],
                                             in1=s_r4[:, cb], op=ALU.mult))
            VEC.wait_ge(m_dve.h, ne)
            n_nb = m_dve.inc(VEC.tensor_scalar(out=s_nb4[:, cb], in0=s_nb4[:, cb],
                                               scalar1=-1.0, scalar2=None, op0=ALU.mult))
            ACT.wait_ge(m_dve.h, n_nb)
            return n_nb

        def transpose_128x2048(src_tile, dst3, bcol):
            nonlocal last_tp_pe
            n_tp = 0
            for half in range(2):
                bank = PB[half]
                TEN.wait_ge(m_dve.h, tp_free[half])
                for g8 in range(8):
                    g = half * 8 + g8
                    last = TEN.transpose(bank[:, g8 * 128:(g8 + 1) * 128],
                                         src_tile[:, g * 128:(g + 1) * 128], ID16[:, :])
                n_tp = m_pe.inc(last)
                VEC.wait_ge(m_pe.h, n_tp)
                dst = dst3[:, half * 8:(half + 1) * 8, bcol * 128:(bcol + 1) * 128]
                src = bank[:, :].rearrange("p (g t) -> p g t", g=8)
                tp_free[half] = m_dve.inc(VEC.tensor_copy(dst, src))
            last_tp_pe = n_tp
            return n_tp, tp_free[1]

        for ti in range(OPTS.get('nta', NTA)):
            warm = (ti == 0)
            n_xld = [0] * NBLK
            for b in range(NBLK):
                GPS.wait_ge(m_act.h, xblk_free[b])
                n_xld[b] = ld_xc.inc(GPS.dma_start(
                    out=xbf4[:, b, :],
                    in_=x_in[ti * TT + b * 128: ti * TT + (b + 1) * 128, :]), 16)
                ACT.wait_ge(ld_xc.h, n_xld[b])
                for s in range(4):
                    sl = xbf4[:, b, s * 512:(s + 1) * 512]
                    ACT.activation(out=LNSCR[:, :], in_=sl, func=AF.Square,
                                   accum_out=s_ssq[:, b * 4 + s: b * 4 + s + 1])
                    ACT.activation(out=LNSCR[:, :], in_=sl, func=AF.Identity,
                                   accum_out=s_ssum[:, b * 4 + s: b * 4 + s + 1])
            n_acc = m_act.inc(ACT.activation(out=LNSCR[:, 0:1], in_=LNSCR[:, 0:1],
                                             func=AF.Identity))
            ln_combine(NBLK, n_acc)

            for b in range(NBLK):
                ACT.wait_ge(m_pe.h, last_tp_pe)
                nnb = m_act.inc(ACT.activation(out=stg[:, :], in_=xbf4[:, b, :],
                                               func=AF.Identity,
                                               scale=s_r4[:, b:b + 1],
                                               bias=s_nb4[:, b:b + 1]))
                xblk_free[b] = nnb
                TEN.wait_ge(m_act.h, nnb)
                transpose_128x2048(stg, xlnfm3, b)
            n_evac = tp_free[1]

            VEC.wait_ge(m_dve.h, n_evac)
            msk = c_mask if warm else c_one
            n_bnd = m_dve.inc(VEC.tensor_scalar(
                out=s_bnd[(ti + 1) % 2][:, :],
                in0=xlnfm3[:, :, 511:512].rearrange("p g o -> p (g o)"),
                scalar1=msk, scalar2=None, op0=ALU.mult))

            SYN.wait_ge(m_dve.h, n_evac)
            for g in range(G):
                last_xln_st = st_a.inc(SYN.dma_start(
                    out=xlnfm_b[g * 128:(g + 1) * 128, ti * TT:(ti + 1) * TT],
                    in_=xlnfm3[:, g, :]), 16)

            VEC.wait_ge(m_pe.h, last_mm_pe)
            bnd = s_bnd[ti % 2]
            x0v = xlnfm3[:, :, 0:1].rearrange("p g o -> p (g o)")
            xk0 = xk3[:, :, 0:1].rearrange("p g o -> p (g o)")
            xv0 = xv3[:, :, 0:1].rearrange("p g o -> p (g o)")
            tiny = [
                lambda: VEC.tensor_tensor(out=s_d0, in0=x0v, in1=bnd, op=ALU.subtract),
                lambda: VEC.tensor_tensor(out=s_dk, in0=s_d0, in1=c_tmk, op=ALU.mult),
                lambda: VEC.tensor_tensor(out=xk0, in0=s_dk, in1=bnd, op=ALU.add),
                lambda: VEC.tensor_tensor(out=s_dk, in0=s_d0, in1=c_tmv, op=ALU.mult),
                lambda: VEC.tensor_tensor(out=xv0, in0=s_dk, in1=bnd, op=ALU.add),
            ]
            ki = 0
            tiny[0]()
            for g in range(G):
                VEC.tensor_tensor(out=t_dif[:, :], in0=xlnfm3[:, g, 1:512],
                                  in1=xlnfm3[:, g, 0:511], op=ALU.subtract)
                VEC.scalar_tensor_tensor(out=xk3[:, g, 1:512], in0=t_dif[:, :],
                                         scalar=c_tmk[:, g:g + 1],
                                         in1=xlnfm3[:, g, 0:511],
                                         op0=ALU.mult, op1=ALU.add)
                last = VEC.scalar_tensor_tensor(out=xv3[:, g, 1:512], in0=t_dif[:, :],
                                                scalar=c_tmv[:, g:g + 1],
                                                in1=xlnfm3[:, g, 0:511],
                                                op0=ALU.mult, op1=ALU.add)
                if ki + 1 < len(tiny):
                    ki += 1
                    tiny[ki]()
            last_mix_gps = m_dve.inc(last)

            for j in range(G):
                jp = j % 2
                kb, vbank = PF[jp], PF[2 + jp]
                TEN.wait_ge(m_dve.h, last_mix_gps)
                if ti == 0 and j == 0:
                    TEN.wait_ge(ld_w.h, n_wA)
                TEN.wait_ge(m_act.h, kb_free[jp])
                for i in range(G):
                    last = TEN.matmul(kb[:, :], wk3[:, i, j * 128:(j + 1) * 128],
                                      xk3[:, i, :], start=(i == 0), stop=(i == G - 1))
                n_k = m_pe.inc(last)
                TEN.wait_ge(m_dve.h, vb_free[jp])
                for i in range(G):
                    last = TEN.matmul(vbank[:, :], wv3[:, i, j * 128:(j + 1) * 128],
                                      xv3[:, i, :], start=(i == 0), stop=(i == G - 1))
                n_v = m_pe.inc(last)
                last_mm_pe = n_v

                ACT.wait_ge(m_pe.h, n_k)
                if j == 0:
                    ACT.wait_ge(st_a.h, last_xln_st)
                    ACT.wait_ge(m_dve.h, last_mix_gps)
                ACT.activation(out=t_q[jp], in_=kb[:, :], func=AF.Exp)
                if warm:
                    n_qe = m_act.inc(ACT.activation(out=LNSCR[:, 0:1], in_=LNSCR[:, 0:1],
                                                    func=AF.Identity))
                else:
                    n_qe = m_act.inc(ACT.activation(out=t_eu[jp], in_=kb[:, :],
                                                    func=AF.Exp, bias=c_u[:, j:j + 1]))
                kb_free[jp] = n_qe

                VEC.wait_ge(m_act.h, n_qe)
                VEC.wait_ge(m_pe.h, n_v)
                if warm:
                    VEC.tensor_scalar(out=t_wq, in0=t_q[jp], scalar1=c_mask,
                                      scalar2=None, op0=ALU.mult)
                    nd1 = m_dve.inc(VEC.tensor_tensor(out=t_p, in0=t_wq,
                                                      in1=vbank[:, :], op=ALU.mult))
                    qsrc = t_wq
                else:
                    VEC.tensor_tensor(out=t_p, in0=t_q[jp], in1=vbank[:, :], op=ALU.mult)
                    nd1 = m_dve.inc(VEC.tensor_tensor(out=t_euv, in0=t_eu[jp],
                                                      in1=vbank[:, :], op=ALU.mult))
                    qsrc = t_q[jp]
                vb_free[jp] = nd1
                lamb = c_lam[:, j:j + 1].broadcast_to((128, TT))
                if not warm:
                    VEC.tensor_copy(t_sa[:, 0:1], s_ca[:, j:j + 1])
                    VEC.tensor_copy(t_sb[:, 0:1], s_cb[:, j:j + 1])
                VEC.tensor_tensor_scan(out=t_sa[:, 1:513], data0=lamb, data1=t_p,
                                       initial=s_ca[:, j:j + 1], op0=ALU.mult, op1=ALU.add)
                VEC.tensor_tensor_scan(out=t_sb[:, 1:513], data0=lamb, data1=qsrc,
                                       initial=s_cb[:, j:j + 1], op0=ALU.mult, op1=ALU.add)
                if not warm:
                    VEC.tensor_tensor(out=t_p, in0=t_sb[:, 0:512], in1=t_eu[jp], op=ALU.add)
                    VEC.tensor_tensor(out=t_q[jp], in0=t_sa[:, 0:512], in1=t_euv, op=ALU.add)
                VEC.tensor_copy(s_ca[:, j:j + 1], t_sa[:, 512:513])
                nd2 = m_dve.inc(VEC.tensor_copy(s_cb[:, j:j + 1], t_sb[:, 512:513]))

                if not warm:
                    ACT.wait_ge(m_dve.h, nd2)
                    ACT.activation(out=t_sa[:, 0:512], in_=t_p, func=AF.Ln)
                    na2 = m_act.inc(ACT.activation(out=t_sb[:, 0:512], in_=t_sa[:, 0:512],
                                                   func=AF.Exp, scale=-1.0))
                    VEC.wait_ge(m_act.h, na2)
                    VEC.wait_ge(st_a.h, wkv_war[jp])
                    nd3 = m_dve.inc(VEC.tensor_tensor(out=t_wkv[jp], in0=t_q[jp],
                                                      in1=t_sb[:, 0:512], op=ALU.mult))
                    ACT.wait_ge(m_dve.h, nd3)
                    wdst = wkv_b[j * 128:(j + 1) * 128, (ti - 1) * TT: ti * TT]
                    wkv_war[jp] = st_a.inc(ACT.dma_start(out=wdst, in_=t_wkv[jp]), 16)

        n_passA_pe = m_pe.n
        n_A_sta = st_a.n

        if UPTO_N >= 1:
            GPS.wait_ge(m_pe.h, n_passA_pe)
            wr3 = wk3
            for i in range(G):
                ld_w.inc(GPS.dma_start(out=wr3[:, i, :], in_=wrT[i * 128:(i + 1) * 128, :]), 16)
            n_wB1 = ld_w.n
            wo3 = wv3
            for i in range(G):
                ld_w.inc(GPS.dma_start(out=wo3[:, i, :], in_=woT[i * 128:(i + 1) * 128, :]), 16)
            n_wB2 = ld_w.n

            xr3 = vb(131072, G * TT).rearrange("p (g t) -> p g t", g=G)
            xlnb3 = vb(147456, G * 513).rearrange("p (g t) -> p g t", g=G)
            B1_tmp = 180224
            b_sr = [vf(B1_tmp, 512), vf(B1_tmp + 2048, 512)]
            b_wld = [vb(B1_tmp + 4096, 512), vb(B1_tmp + 5120, 512)]
            b_wsr = [vb(B1_tmp + 6144, 512), vb(B1_tmp + 7168, 512)]
            b_dif = vb(B1_tmp + 8192, 512)

            ACT.wait_ge(m_pe.h, n_passA_pe)
            ACT.wait_ge(st_a.h, n_A_sta)
            ACT.preload_act_table(AF.Sigmoid)

            xlnb_free = 0
            rb_free = [0, 0]
            wld_free = [0, 0]
            wsr_free = [0, 0]
            wkv_ld = [0] * G
            xlnsrc = xlnfm_b.rearrange("(g p) t -> p g t", p=128)

            for tj in range(NTB):
                col0 = W + tj * TT - 1
                SYN.wait_ge(m_dve.h, xlnb_free)
                if tj == 0:
                    SYN.wait_ge(st_a.h, n_A_sta)
                for g in range(G):
                    last_ld = ld_b.inc(SYN.dma_start(out=xlnb3[:, g, :],
                                                     in_=xlnsrc[:, g, col0:col0 + 513]), 16)
                VEC.wait_ge(ld_b.h, last_ld)
                if tj == 0:
                    x0 = xlnb3[:, :, 0:1].rearrange("p g o -> p (g o)")
                    ne = m_dve.inc(VEC.tensor_tensor(out=s_dk, in0=x0,
                                                     in1=c_mask.broadcast_to((128, 16)),
                                                     op=ALU.mult))
                    VEC.wait_ge(m_dve.h, ne)
                    ne = m_dve.inc(VEC.tensor_copy(x0, s_dk))
                    VEC.wait_ge(m_dve.h, ne)
                VEC.wait_ge(m_pe.h, last_mm_pe)
                for g in range(G):
                    VEC.tensor_tensor(out=b_dif[:, :], in0=xlnb3[:, g, 1:513],
                                      in1=xlnb3[:, g, 0:512], op=ALU.subtract)
                    last = VEC.scalar_tensor_tensor(out=xr3[:, g, :], in0=b_dif[:, :],
                                                    scalar=c_tmr[:, g:g + 1],
                                                    in1=xlnb3[:, g, 0:512],
                                                    op0=ALU.mult, op1=ALU.add)
                xlnb_free = m_dve.inc(last)

                for j in range(G):
                    jp = j % 2
                    rb = PF[jp]
                    TEN.wait_ge(m_dve.h, xlnb_free)
                    if tj == 0 and j == 0:
                        TEN.wait_ge(ld_w.h, n_wB1)
                    TEN.wait_ge(m_act.h, rb_free[jp])
                    for i in range(G):
                        last = TEN.matmul(rb[:, :], wr3[:, i, j * 128:(j + 1) * 128],
                                          xr3[:, i, :], start=(i == 0), stop=(i == G - 1))
                    n_r = m_pe.inc(last)
                    last_mm_pe = n_r
                    for jj in ([0, 1] if j == 0 else ([j + 1] if j + 1 < G else [])):
                        ACT.wait_ge(m_dve.h, wld_free[jj % 2])
                        wsrc = wkv_b[jj * 128:(jj + 1) * 128, tj * TT:(tj + 1) * TT]
                        wkv_ld[jj] = ld_b.inc(ACT.dma_start(out=b_wld[jj % 2], in_=wsrc), 16)
                    ACT.wait_ge(m_pe.h, n_r)
                    ACT.wait_ge(m_dve.h, wld_free[jp])
                    na_sr = m_act.inc(ACT.activation(out=b_sr[jp], in_=rb[:, :],
                                                     func=AF.Sigmoid))
                    rb_free[jp] = na_sr
                    VEC.wait_ge(m_act.h, na_sr)
                    VEC.wait_ge(ld_b.h, wkv_ld[j])
                    VEC.wait_ge(st_b.h, wsr_free[jp])
                    nd = m_dve.inc(VEC.tensor_tensor(out=b_wsr[jp], in0=b_wld[jp],
                                                     in1=b_sr[jp], op=ALU.mult))
                    wld_free[jp] = nd
                    SYN.wait_ge(m_dve.h, nd)
                    wdst = wkvsr_b[j * 128:(j + 1) * 128, tj * TT:(tj + 1) * TT]
                    wsr_free[jp] = st_b.inc(SYN.dma_start(out=wdst, in_=b_wsr[jp]), 16)

            n_B1_pe = m_pe.n
            n_B1_stb = st_b.n

        if UPTO_N >= 2:
            xb2 = vf(0, 2 * D).rearrange("p (b d) -> p b d", b=2)
            x1b2 = vf(16384, 2 * D).rearrange("p (b d) -> p b d", b=2)
            lgt = vf(32768, 512)
            lgtm = vf(34816, 32)
            selT = vf(34944, 512)
            posv = vf(36992, 512)
            rows1 = vb(40960, D)
            rows2 = vb(45056, D)
            xtb4 = vb(49152, 4 * D).rearrange("p (b d) -> p b d", b=4)
            wsr3 = vb(131072, G * TT).rearrange("p (g t) -> p g t", g=G)
            hi3 = vb(147456, G * TT).rearrange("p (g t) -> p g t", g=G)
            lo3 = vb(163840, G * TT).rearrange("p (g t) -> p g t", g=G)
            xt2 = vf(180224, 2 * D).rearrange("p (b d) -> p b d", b=2)
            lot = vb(196608, D)
            gwhi3 = GWHI.rearrange("p (i e) -> p i e", i=G)
            gwlo3 = GWLO.rearrange("p (i e) -> p i e", i=G)

            x_free = [0, 0]
            x1_free = [0, 0]
            xt_free = [0, 0]
            ob_free = [0, 0]
            gb_free = 0
            lb_free = 0
            pb4_free = 0
            hi_free_pe = 0
            rows_free = [0, 0]
            xtb_sc_free = [0] * 4
            wsr_ld_done = 0
            lot_tp_pe = 0

            ACT.wait_ge(m_pe.h, n_B1_pe)
            load_lnexp_table()

            for tj in range(NTB):
                ACT.wait_ge(m_pe.h, last_mm_pe)
                if tj == 0:
                    ACT.wait_ge(st_b.h, n_B1_stb)
                for g in range(G):
                    wsrc = wkvsr_b[g * 128:(g + 1) * 128, tj * TT:(tj + 1) * TT]
                    wsr_ld_done = ld_b.inc(ACT.dma_start(out=wsr3[:, g, :], in_=wsrc), 16)

                for b in range(NBLK):
                    par = b % 2
                    SYN.wait_ge(m_dve.h, x_free[par])
                    if tj == 0 and b == 0:
                        SYN.wait_ge(m_pe.h, n_B1_pe)
                    n_xb = ld_b.inc(SYN.dma_start(
                        out=xb2[:, par, :],
                        in_=x_in[W + tj * TT + b * 128: W + tj * TT + (b + 1) * 128, :]), 16)
                    n_x1 = 0
                    for ds in range(4):
                        ob = PF[ds % 2]
                        TEN.wait_ge(ld_b.h, wsr_ld_done)
                        if tj == 0 and b == 0 and ds == 0:
                            TEN.wait_ge(ld_w.h, n_wB2)
                        TEN.wait_ge(m_dve.h, ob_free[ds % 2])
                        for i in range(G):
                            last = TEN.matmul(ob[:, :], wsr3[:, i, b * 128:(b + 1) * 128],
                                              wo3[:, i, ds * 512:(ds + 1) * 512],
                                              start=(i == 0), stop=(i == G - 1))
                        n_o = m_pe.inc(last)
                        last_mm_pe = n_o
                        VEC.wait_ge(m_pe.h, n_o)
                        VEC.wait_ge(ld_b.h, n_xb)
                        if ds == 0:
                            VEC.wait_ge(st_b.h, x1_free[par])
                            VEC.wait_ge(m_act.h, xt_free[par])
                        n_x1 = m_dve.inc(VEC.tensor_tensor(
                            out=x1b2[:, par, ds * 512:(ds + 1) * 512], in0=ob[:, :],
                            in1=xb2[:, par, ds * 512:(ds + 1) * 512], op=ALU.add))
                        ob_free[ds % 2] = n_x1
                    x_free[par] = n_x1
                    SYN.wait_ge(m_dve.h, n_x1)
                    x1_free[par] = st_b.inc(SYN.dma_start(
                        out=x1_b[tj * TT + b * 128: tj * TT + (b + 1) * 128, :],
                        in_=x1b2[:, par, :]), 16)
                    ACT.wait_ge(m_dve.h, n_x1)
                    n_acc = ln_stats_tile([x1b2[:, par, :]])
                    ln_combine(1, n_acc)
                    na_xt = m_act.inc(ACT.activation(out=xt2[:, par, :], in_=x1b2[:, par, :],
                                                     func=AF.Identity,
                                                     scale=s_r4[:, 0:1], bias=s_nb4[:, 0:1]))
                    xt_free[par] = na_xt
                    VEC.wait_ge(m_act.h, na_xt)
                    VEC.wait_ge(sc_r.h, xtb_sc_free[b])
                    ne = m_dve.inc(VEC.tensor_copy(xtb4[:, b, :], xt2[:, par, :]))
                    VEC.wait_ge(m_dve.h, ne)
                    VEC.wait_ge(m_pe.h, lot_tp_pe)
                    n_lo = m_dve.inc(VEC.tensor_tensor(out=lot[:, :], in0=xt2[:, par, :],
                                                       in1=xtb4[:, b, :], op=ALU.subtract))
                    ACT.wait_ge(m_dve.h, ne)
                    st_b.inc(ACT.dma_start(
                        out=xt_b[tj * TT + b * 128: tj * TT + (b + 1) * 128, :],
                        in_=xtb4[:, b, :]), 16)
                    TEN.wait_ge(m_dve.h, n_lo)
                    TEN.wait_ge(m_pe.h, hi_free_pe)
                    transpose_128x2048(xtb4[:, b, :], hi3, b)
                    npe, _ = transpose_128x2048(lot, lo3, b)
                    lot_tp_pe = npe
                n_hilo = max(tp_free)

                gbank = PF[4]
                TEN.wait_ge(m_dve.h, n_hilo)
                TEN.wait_ge(m_dve.h, gb_free)
                TEN.wait_ge(m_dve.h, pb4_free)
                first = True
                for pi, (a3, g3) in enumerate(((hi3, gwhi3), (hi3, gwlo3), (lo3, gwhi3))):
                    for i in range(G):
                        last = TEN.matmul(gbank[0:8, :], g3[:, i, :], a3[:, i, :],
                                          start=first, stop=(pi == 2 and i == G - 1))
                        first = False
                n_gate = m_pe.inc(last)
                hi_free_pe = n_gate
                VEC.wait_ge(m_pe.h, n_gate)
                gb_free = m_dve.inc(VEC.tensor_copy(lgt[0:8, :], gbank[0:8, :]))
                lbank = PF[5]
                TEN.wait_ge(m_dve.h, gb_free)
                TEN.wait_ge(m_dve.h, lb_free)
                for b in range(NBLK):
                    last = TEN.transpose(lbank[:, b * 8:(b + 1) * 8],
                                         lgt[0:8, b * 128:(b + 1) * 128], ID32[0:8, 0:8])
                n_lt = m_pe.inc(last)
                VEC.wait_ge(m_pe.h, n_lt)
                lb_free = m_dve.inc(VEC.tensor_copy(lgtm[:, :], lbank[:, 0:32]))

                VEC.wait_ge(m_dve.h, lb_free)
                lgb = [lgtm[:, b * 8:(b + 1) * 8] for b in range(NBLK)]

                def ve(instr):
                    nn = m_dve.inc(instr)
                    VEC.wait_ge(m_dve.h, nn)
                    return nn

                ne = 0
                for b in range(NBLK):
                    ne = m_dve.inc(VEC.tensor_reduce(out=s_m12[:, b:b + 1], in_=lgb[b],
                                                     axis=AXX, op=ALU.max))
                VEC.wait_ge(m_dve.h, ne)
                for b in range(NBLK):
                    ve(VEC.tensor_scalar(out=r_eq, in0=lgb[b], scalar1=s_m12[:, b:b + 1],
                                         scalar2=None, op0=ALU.is_ge))
                    ve(VEC.scalar_tensor_tensor(out=r_cand, in0=r_eq,
                                                scalar=-BIG, in1=c_iotaB,
                                                op0=ALU.mult, op1=ALU.add))
                    ve(VEC.tensor_reduce(out=s_i1[:, b:b + 1], in_=r_cand,
                                         axis=AXX, op=ALU.min))
                    ve(VEC.tensor_scalar(out=r_msk1[b], in0=c_iota, scalar1=s_i1[:, b:b + 1],
                                         scalar2=None, op0=ALU.is_equal))
                    ve(VEC.scalar_tensor_tensor(out=r_lg2, in0=r_msk1[b],
                                                scalar=-BIG, in1=lgb[b],
                                                op0=ALU.mult, op1=ALU.add))
                    ve(VEC.tensor_reduce(out=s_m12[:, 4 + b:5 + b], in_=r_lg2,
                                         axis=AXX, op=ALU.max))
                    ve(VEC.tensor_scalar(out=r_eq, in0=r_lg2, scalar1=s_m12[:, 4 + b:5 + b],
                                         scalar2=None, op0=ALU.is_ge))
                    ve(VEC.scalar_tensor_tensor(out=r_cand, in0=r_eq,
                                                scalar=-BIG, in1=c_iotaB,
                                                op0=ALU.mult, op1=ALU.add))
                    ve(VEC.tensor_reduce(out=s_i2[:, b:b + 1], in_=r_cand,
                                         axis=AXX, op=ALU.min))
                    ne = ve(VEC.tensor_scalar(out=r_msk2[b], in0=c_iota,
                                              scalar1=s_i2[:, b:b + 1],
                                              scalar2=None, op0=ALU.is_equal))
                ACT.wait_ge(m_dve.h, ne)
                na = m_act.inc(ACT.activation(out=s_s12, in_=s_m12, func=AF.Exp, scale=-1.0))
                VEC.wait_ge(m_act.h, na)
                ne = m_dve.inc(VEC.tensor_scalar(out=s_s12, in0=s_s12, scalar1=1.0,
                                                 scalar2=None, op0=ALU.add))
                VEC.wait_ge(m_dve.h, ne)
                ne = m_dve.inc(VEC.reciprocal(s_s12, s_s12))
                VEC.wait_ge(m_dve.h, ne)

                selbank = PF[5]
                TEN.wait_ge(m_dve.h, lb_free)
                npt = 0
                for b in range(NBLK):
                    if b > 0:
                        VEC.wait_ge(m_pe.h, npt)
                    nsel = m_dve.inc(VEC.tensor_tensor(out=r_ptm, in0=r_msk1[b],
                                                       in1=r_msk2[b], op=ALU.add))
                    TEN.wait_ge(m_dve.h, nsel)
                    npt = m_pe.inc(TEN.transpose(selbank[0:8, b * 128:(b + 1) * 128],
                                                 r_ptm, ID32[:, :]))
                VEC.wait_ge(m_pe.h, npt)
                lb_free = m_dve.inc(VEC.tensor_copy(selT[0:8, :], selbank[0:8, :]))
                VEC.wait_ge(m_dve.h, lb_free)
                onesb = c_one[0:8, 0:1].broadcast_to((8, TT))
                nps = m_dve.inc(VEC.tensor_tensor_scan(out=posv[0:8, :], data0=onesb,
                                                       data1=selT[0:8, :],
                                                       initial=s_run[0:8, 0:1],
                                                       op0=ALU.mult, op1=ALU.add))
                pbank = PF[4]
                TEN.wait_ge(m_dve.h, nps)
                TEN.wait_ge(m_dve.h, gb_free)
                for b in range(NBLK):
                    npt = m_pe.inc(TEN.transpose(pbank[:, b * 8:(b + 1) * 8],
                                                 posv[0:8, b * 128:(b + 1) * 128],
                                                 ID32[0:8, 0:8]))
                VEC.wait_ge(m_pe.h, npt)
                npos = m_dve.inc(VEC.tensor_copy(lgtm[:, :], pbank[:, 0:32]))
                pb4_free = npos
                VEC.wait_ge(m_dve.h, npos)
                nrun = m_dve.inc(VEC.tensor_copy(s_run[0:8, 0:1], posv[0:8, 511:512]))
                VEC.wait_ge(m_dve.h, nrun)

                for b in range(NBLK):
                    ptm = lgtm[:, b * 8:(b + 1) * 8]
                    ve(VEC.tensor_tensor(out=r_eq, in0=r_msk1[b], in1=ptm, op=ALU.mult))
                    ve(VEC.tensor_tensor(out=r_lg2, in0=r_msk2[b], in1=ptm, op=ALU.mult))
                    ve(VEC.tensor_reduce(out=r_pa[:, b:b + 1], in_=r_eq, axis=AXX, op=ALU.add))
                    ve(VEC.tensor_reduce(out=r_pb[:, b:b + 1], in_=r_lg2, axis=AXX, op=ALU.add))
                    ve(VEC.tensor_scalar(out=r_pac[:, b:b + 1], in0=r_pa[:, b:b + 1],
                                         scalar1=float(CAP), scalar2=None, op0=ALU.min))
                    ve(VEC.tensor_scalar(out=r_pbc[:, b:b + 1], in0=r_pb[:, b:b + 1],
                                         scalar1=float(CAP), scalar2=None, op0=ALU.min))
                    ve(VEC.scalar_tensor_tensor(out=r_sl[:, 0:1], in0=s_i1[:, b:b + 1],
                                                scalar=float(CAP), in1=r_pac[:, b:b + 1],
                                                op0=ALU.mult, op1=ALU.add))
                    ve(VEC.scalar_tensor_tensor(out=r_sl[:, 1:2], in0=s_i2[:, b:b + 1],
                                                scalar=float(CAP), in1=r_pbc[:, b:b + 1],
                                                op0=ALU.mult, op1=ALU.add))
                    ve(VEC.tensor_scalar(out=r_sl2[:, 0:2], in0=r_sl[:, 0:2],
                                         scalar1=1.0, scalar2=-1.0,
                                         op0=ALU.mult, op1=ALU.add))
                    cidx = tj * 4 + b
                    VEC.tensor_copy(i_slotA[:, cidx:cidx + 1], r_sl2[:, 0:1])
                    ne = ve(VEC.tensor_copy(i_slotB[:, cidx:cidx + 1], r_sl2[:, 1:2]))
                    VEC.wait_ge(sc_r.h, rows_free[0])
                    m_dve.inc(VEC.tensor_scalar(out=rows1[:, :], in0=xtb4[:, b, :],
                                                scalar1=s_s12[:, b:b + 1],
                                                scalar2=None, op0=ALU.mult))
                    VEC.wait_ge(sc_r.h, rows_free[1])
                    ne2 = m_dve.inc(VEC.tensor_scalar(out=rows2[:, :], in0=xtb4[:, b, :],
                                                      scalar1=s_s12[:, 4 + b:5 + b],
                                                      scalar2=None, op0=ALU.mult))
                    GPS.wait_ge(m_dve.h, ne2)
                    if OPTS.get("no_scatter"):
                        rows_free[0] = sc_r.inc(GPS.dma_start(
                            out=routed_b[cidx * 128:(cidx + 1) * 128, :], in_=rows1[:, :]), 16)
                        rows_free[1] = sc_r.inc(GPS.dma_start(
                            out=routed_b[4096 + cidx * 128: 4096 + (cidx + 1) * 128, :],
                            in_=rows2[:, :]), 16)
                    else:
                        rows_free[0] = sc_r.inc(GPS.indirect_dma_start(
                            out=routed_b[:, :], out_offset=IOA(ap=i_slotA[:, cidx:cidx + 1], axis=0),
                            in_=rows1[:, :], in_offset=None), 16)
                        rows_free[1] = sc_r.inc(GPS.indirect_dma_start(
                            out=routed_b[:, :], out_offset=IOA(ap=i_slotB[:, cidx:cidx + 1], axis=0),
                            in_=rows2[:, :], in_offset=None), 16)
                    xtb_sc_free[b] = rows_free[1]
                if debug and tj == OPTS.get('dumptile', NTB - 1):
                    SYN.wait_ge(m_dve.h, m_dve.n)
                    st_b.inc(SYN.dma_start(out=dbg_idx[:, :], in_=IDXS[:, :]), 16)
                    st_b.inc(SYN.dma_start(out=dbg_stat[:, :], in_=STAT[:, :]), 16)
                    st_b.inc(SYN.dma_start(out=dbg_rsc[:, :], in_=RSC[:, :]), 16)
                    st_b.inc(SYN.dma_start(out=dbg_ar[:, :], in_=vf(32768, 4096)), 16)

            n_B2_pe = m_pe.n
            n_B2_sc = sc_r.n
            n_B2_dve = m_dve.n
            n_B2_stb = st_b.n

        if UPTO_N >= 3:
            HOME = [0, 98304]
            C_w3, C_w2 = 32768, 65536
            rowsLO3 = vb(131072, 4 * D).rearrange("p (b d) -> p b d", b=4)
            rfm3 = vb(147456, G * TT).rearrange("p (g t) -> p g t", g=G)
            h3v = vb(163840, HG * TT).rearrange("p (h t) -> p h t", h=HG)
            hs2 = [vb(172032, 512), vb(173056, 512)]
            orst8 = vb(174080, 8 * 512).rearrange("p (d t) -> p d t", d=8)

            h1_free = [0, 0]
            h3_free = [0, 0]
            oc_free = [0, 0]
            otr_pe = 0
            rows_lo_free = 0
            w1_used_pe = [0, 0]
            w3_used_pe = 0
            w2_used_pe = 0

            def load_w_hid(dst3, src, nchunks, gate_pe):
                GPS.wait_ge(m_pe.h, gate_pe)
                nn = 0
                for i in range(nchunks):
                    nn = ld_w.inc(GPS.dma_start(out=dst3[:, i, :],
                                                in_=src[i * 128:(i + 1) * 128, :]), 16)
                return nn

            specs = [dict(kind="exp", e=e, sizes=[512, 512, 256],
                          rbase=e * CAP, cbase=e * CAP) for e in range(E)]
            specs.append(dict(kind="sh", e=None, sizes=[512] * 8, rbase=0, cbase=E * CAP))

            GPS.wait_ge(sc_r.h, n_B2_sc)
            GPS.wait_ge(m_dve.h, n_B2_dve)
            GPS.wait_ge(m_pe.h, n_B2_pe)

            w1ld = [0, 0]
            first_silu = True
            for ei, sp in enumerate(specs):
                hm = ei % 2
                w1v = vb(HOME[hm], G * HID).rearrange("p (i h) -> p i h", i=G)
                w3v = vb(C_w3, G * HID).rearrange("p (i h) -> p i h", i=G)
                w2v = vb(C_w2, HG * D).rearrange("p (k d) -> p k d", k=HG)
                if ei == 0:
                    w1ld[0] = load_w_hid(w1v, w1T[0], G, 0)
                if sp["kind"] == "exp":
                    w3src, w2src = w3T[sp["e"]], w2T[sp["e"]]
                else:
                    w3src, w2src = sw3T, sw2T
                w3ld = load_w_hid(w3v, w3src, G, w3_used_pe)
                w2ld = load_w_hid(w2v, w2src, HG, w2_used_pe)
                if ei + 1 < len(specs):
                    nsp = specs[ei + 1]
                    w1nv = vb(HOME[(ei + 1) % 2], G * HID).rearrange("p (i h) -> p i h", i=G)
                    nsrc = w1T[nsp["e"]] if nsp["kind"] == "exp" else sw1T
                    w1ld[(ei + 1) % 2] = load_w_hid(w1nv, nsrc, G, w1_used_pe[(ei + 1) % 2])

                for st_i, n in enumerate(sp["sizes"]):
                    nb = n // 128
                    row0 = sp["rbase"] + st_i * 512
                    SYN.wait_ge(st_c.h, rows_lo_free)
                    if sp["kind"] == "sh" and st_i == 0:
                        SYN.wait_ge(st_b.h, n_B2_stb)
                    src = routed_b if sp["kind"] == "exp" else xt_b
                    src3 = src[row0:row0 + n, :].rearrange("(b p) d -> p b d", p=128)
                    n_rl = ld_c.inc(SYN.dma_start(out=rowsLO3[:, 0:nb, :], in_=src3), 16)
                    n_tin = 0
                    for bb in range(nb):
                        TEN.wait_ge(ld_c.h, n_rl)
                        _, n_tin = transpose_128x2048(rowsLO3[:, bb, :], rfm3, bb)
                    for hb in range(HG):
                        hp = hb % 2
                        pA, pC = PF[hp], PF[2 + hp]
                        TEN.wait_ge(m_dve.h, n_tin)
                        TEN.wait_ge(ld_w.h, w1ld[hm])
                        TEN.wait_ge(m_act.h, h1_free[hp])
                        for i in range(G):
                            last = TEN.matmul(pA[:, 0:n], w1v[:, i, hb * 128:(hb + 1) * 128],
                                              rfm3[:, i, 0:n], start=(i == 0), stop=(i == G - 1))
                        n1 = m_pe.inc(last)
                        w1_used_pe[hm] = n1
                        TEN.wait_ge(ld_w.h, w3ld)
                        TEN.wait_ge(m_dve.h, h3_free[hp])
                        for i in range(G):
                            last = TEN.matmul(pC[:, 0:n], w3v[:, i, hb * 128:(hb + 1) * 128],
                                              rfm3[:, i, 0:n], start=(i == 0), stop=(i == G - 1))
                        n3 = m_pe.inc(last)
                        w3_used_pe = n3
                        last_mm_pe = n3
                        ACT.wait_ge(m_pe.h, n1)
                        ACT.wait_ge(m_dve.h, h3_free[hp])
                        if first_silu:
                            ACT.preload_act_table(AF.Silu)
                            first_silu = False
                        na = m_act.inc(ACT.activation(out=hs2[hp][:, 0:n], in_=pA[:, 0:n],
                                                      func=AF.Silu))
                        h1_free[hp] = na
                        VEC.wait_ge(m_act.h, na)
                        VEC.wait_ge(m_pe.h, n3)
                        nd = m_dve.inc(VEC.tensor_tensor(out=h3v[:, hb, 0:n],
                                                         in0=hs2[hp][:, 0:n],
                                                         in1=pC[:, 0:n], op=ALU.mult))
                        h3_free[hp] = nd
                    n_h = max(h3_free)
                    for dh in range(2):
                        n_oc = 0
                        for d8 in range(8):
                            db = dh * 8 + d8
                            pO = PF[4 + d8 % 2]
                            TEN.wait_ge(m_dve.h, n_h)
                            TEN.wait_ge(ld_w.h, w2ld)
                            TEN.wait_ge(m_dve.h, oc_free[d8 % 2])
                            for k in range(HG):
                                last = TEN.matmul(pO[:, 0:n], w2v[:, k, db * 128:(db + 1) * 128],
                                                  h3v[:, k, 0:n], start=(k == 0),
                                                  stop=(k == HG - 1))
                            n_o = m_pe.inc(last)
                            w2_used_pe = n_o
                            last_mm_pe = n_o
                            VEC.wait_ge(m_pe.h, n_o)
                            VEC.wait_ge(m_pe.h, otr_pe)
                            n_oc = m_dve.inc(VEC.tensor_copy(orst8[:, d8, 0:n], pO[:, 0:n]))
                            oc_free[d8 % 2] = n_oc
                        for bb in range(nb):
                            half = bb % 2
                            bank = PB[half]
                            TEN.wait_ge(m_dve.h, n_oc)
                            TEN.wait_ge(m_dve.h, tp_free[half])
                            for d8 in range(8):
                                last = TEN.transpose(bank[:, d8 * 128:(d8 + 1) * 128],
                                                     orst8[:, d8, bb * 128:(bb + 1) * 128],
                                                     ID16[:, :])
                            n_tp = m_pe.inc(last)
                            otr_pe = n_tp
                            VEC.wait_ge(m_pe.h, n_tp)
                            dst = rowsLO3[:, bb, dh * 1024:(dh + 1) * 1024]
                            tp_free[half] = m_dve.inc(VEC.tensor_copy(dst, bank[:, :]))
                    n_ot = max(tp_free)
                    SYN.wait_ge(m_dve.h, n_ot)
                    cdst = contrib_b[sp["cbase"] + st_i * 512: sp["cbase"] + st_i * 512 + n, :]
                    cdst3 = cdst.rearrange("(b p) d -> p b d", p=128)
                    rows_lo_free = st_c.inc(SYN.dma_start(out=cdst3, in_=rowsLO3[:, 0:nb, :]), 16)

            n_C_pe = m_pe.n
            n_C_st = st_c.n

        if UPTO_N >= 4:
            d_x1 = vf(0, 2 * D).rearrange("p (b d) -> p b d", b=2)
            d_sh = vb(16384, 2 * D).rearrange("p (b d) -> p b d", b=2)
            d_gA = vb(24576, 2 * D).rearrange("p (b d) -> p b d", b=2)
            d_gB = vb(32768, 2 * D).rearrange("p (b d) -> p b d", b=2)
            d_o1 = vf(40960, 2 * D).rearrange("p (b d) -> p b d", b=2)
            d_o2 = vf(57344, 2 * D).rearrange("p (b d) -> p b d", b=2)
            d_o3 = vf(73728, 2 * D).rearrange("p (b d) -> p b d", b=2)

            d1_free = [0, 0]
            d2_free = [0, 0]
            d3_free = [0, 0]
            o3_free = [0, 0]

            for n in range(TOK // 128):
                par = n % 2
                SYN.wait_ge(m_dve.h, d1_free[par])
                if n == 0:
                    SYN.wait_ge(m_pe.h, n_C_pe)
                    SYN.wait_ge(st_c.h, n_C_st)
                ld_d.inc(SYN.dma_start(out=d_x1[:, par, :],
                                       in_=x1_b[n * 128:(n + 1) * 128, :]), 16)
                n_l1 = ld_d.inc(SYN.dma_start(
                    out=d_sh[:, par, :],
                    in_=contrib_b[E * CAP + n * 128: E * CAP + (n + 1) * 128, :]), 16)
                GPS.wait_ge(m_gps.h, d2_free[par])
                if n == 0:
                    GPS.wait_ge(st_c.h, n_C_st)
                if OPTS.get("no_gather"):
                    ld_g.inc(GPS.dma_start(out=d_gA[:, par, :],
                                           in_=contrib_b[n * 128:(n + 1) * 128, :]), 16)
                    n_g = ld_g.inc(GPS.dma_start(out=d_gB[:, par, :],
                                                 in_=contrib_b[n * 128:(n + 1) * 128, :]), 16)
                else:
                    ld_g.inc(GPS.indirect_dma_start(out=d_gA[:, par, :], out_offset=None,
                                                    in_=contrib_b[:, :],
                                                    in_offset=IOA(ap=i_slotA[:, n:n + 1], axis=0)), 16)
                    n_g = ld_g.inc(GPS.indirect_dma_start(out=d_gB[:, par, :], out_offset=None,
                                                          in_=contrib_b[:, :],
                                                          in_offset=IOA(ap=i_slotB[:, n:n + 1],
                                                                        axis=0)), 16)
                VEC.wait_ge(ld_d.h, n_l1)
                VEC.wait_ge(st_o.h, d3_free[par])
                no1 = m_dve.inc(VEC.tensor_tensor(out=d_o1[:, par, :], in0=d_x1[:, par, :],
                                                  in1=d_sh[:, par, :], op=ALU.add))
                d1_free[par] = no1
                GPS.wait_ge(ld_g.h, n_g)
                GPS.wait_ge(m_dve.h, o3_free[par])
                no2 = m_gps.inc(GPS.tensor_tensor(out=d_o2[:, par, :], in0=d_gA[:, par, :],
                                                  in1=d_gB[:, par, :], op=ALU.add))
                d2_free[par] = no2
                VEC.wait_ge(m_dve.h, no1)
                VEC.wait_ge(m_gps.h, no2)
                no3 = m_dve.inc(VEC.tensor_tensor(out=d_o3[:, par, :], in0=d_o1[:, par, :],
                                                  in1=d_o2[:, par, :], op=ALU.add))
                o3_free[par] = no3
                SYN.wait_ge(m_dve.h, no3)
                d3_free[par] = st_o.inc(SYN.dma_start(out=out_t[n * 128:(n + 1) * 128, :],
                                                      in_=d_o3[:, par, :]), 16)

            SYN.wait_ge(st_o.h, st_o.n)
        if UPTO_N < 4:
            endsem = {0: st_a, 1: st_b, 2: sc_r, 3: st_c}[UPTO_N]
            SYN.wait_ge(endsem.h, endsem.n)
            if UPTO_N == 2:
                SYN.wait_ge(st_b.h, st_b.n)

    return nc


def _pack_chan(v):
    return np.ascontiguousarray(np.asarray(v, np.float32).reshape(G, 128).T)


def make_in_maps(inp):
    x = np.asarray(inp["L0_attn_x"], np.float32)
    maps = []
    wkTa = np.ascontiguousarray(np.asarray(inp["L0_attn_Wk"], np.float32).T)
    wvTa = np.ascontiguousarray(np.asarray(inp["L0_attn_Wv"], np.float32).T)
    wrTa = np.ascontiguousarray(np.asarray(inp["L0_attn_Wr"], np.float32).T)
    woTa = np.ascontiguousarray(np.asarray(inp["L0_attn_Wo"], np.float32).T)
    w1Ta = np.ascontiguousarray(np.asarray(inp["L0_ffn_w1"], np.float32).transpose(0, 2, 1))
    w2Ta = np.ascontiguousarray(np.asarray(inp["L0_ffn_w2"], np.float32).transpose(0, 2, 1))
    w3Ta = np.ascontiguousarray(np.asarray(inp["L0_ffn_w3"], np.float32).transpose(0, 2, 1))
    sw1Ta = np.ascontiguousarray(np.asarray(inp["L0_ffn_sw1"], np.float32).T)
    sw2Ta = np.ascontiguousarray(np.asarray(inp["L0_ffn_sw2"], np.float32).T)
    sw3Ta = np.ascontiguousarray(np.asarray(inp["L0_ffn_sw3"], np.float32).T)
    gwT = np.asarray(inp["L0_ffn_gate_w"], np.float32).T
    gw_pk = np.ascontiguousarray(gwT.reshape(G, 128, E).transpose(1, 0, 2).reshape(128, G * E))
    tmk = _pack_chan(np.asarray(inp["L0_attn_time_mix_k"]).reshape(-1))
    tmv = _pack_chan(np.asarray(inp["L0_attn_time_mix_v"]).reshape(-1))
    tmr = _pack_chan(np.asarray(inp["L0_attn_time_mix_r"]).reshape(-1))
    dec = _pack_chan(np.asarray(inp["L0_attn_time_decay"]).reshape(-1))
    tf = _pack_chan(np.asarray(inp["L0_attn_time_first"]).reshape(-1))
    for c in range(8):
        b = c // 4
        t0 = (c % 4) * TOK
        if t0 == 0:
            xs = np.concatenate([x[b, 0:W], x[b, 0:TOK]], axis=0)
            mk = np.zeros((128, 1), np.float32)
        else:
            xs = x[b, t0 - W: t0 + TOK]
            mk = np.ones((128, 1), np.float32)
        maps.append({
            "x": np.ascontiguousarray(xs),
            "wkT": wkTa, "wvT": wvTa, "wrT": wrTa, "woT": woTa,
            "w1T": w1Ta, "w2T": w2Ta, "w3T": w3Ta,
            "sw1T": sw1Ta, "sw2T": sw2Ta, "sw3T": sw3Ta,
            "gw_pk": gw_pk, "tmk": tmk, "tmv": tmv, "tmr": tmr,
            "decay": dec, "tfirst": tf, "mask": mk,
        })
    return maps


def kernel(**inp):
    maps = make_in_maps(inp)
    nc = build(debug=False)
    res = run_bass_kernel_spmd(nc, maps, core_ids=list(range(8)))
    out = np.empty((2, 16384, D), np.float32)
    for c in range(8):
        b = c // 4
        t0 = (c % 4) * TOK
        out[b, t0:t0 + TOK] = res.results[c]["out"]
    return out
